# version 26
# baseline (speedup 1.0000x reference)
import math
from contextlib import ExitStack

import numpy as np
import concourse.bass as bass
import concourse.mybir as mybir
from concourse.bass_utils import run_bass_kernel_spmd

F32 = mybir.dt.float32
BF16 = mybir.dt.bfloat16
AF = mybir.ActivationFunctionType
ALU = mybir.AluOpType

B, S, D = 4, 4096, 1024
HALF = 2048
NJ = 16
ND = 8
NE = 8
SCALE = math.sqrt(0.5)
EPS = 1e-5
RD = 1.0 / D

SEM_NAMES = (["sdx", "sdxp", "sdxp1", "sdxp2", "sdxp3", "sdsm", "sdwm", "sdwm1", "sdo", "sdo1", "sdo2", "sdo3", "sg", "sc", "st", "sv"]
             + ["swe%d" % e for e in range(8)])


class Plan:
    emitting = False

    def __init__(self):
        self.ctr = dict.fromkeys(SEM_NAMES, 0)
        self.val = {}

    def inc(self, inst, sem, n, key=None):
        self.ctr[sem] += n
        if key is not None:
            assert key not in self.val, f"dup key {key}"
            self.val[key] = self.ctr[sem]

    def wait(self, eng, sem, key):
        pass

    def waitv(self, eng, sem, v):
        pass


class Emit:
    emitting = True

    def __init__(self, plan, sems):
        self.val = plan.val
        self.sems = sems

    def inc(self, inst, sem, n, key=None):
        inst.then_inc(self.sems[sem], n)

    def wait(self, eng, sem, key):
        eng.wait_ge(self.sems[sem], self.val[key])

    def waitv(self, eng, sem, v):
        eng.wait_ge(self.sems[sem], v)


class DummyEng:
    def __getattr__(self, name):
        return lambda *a, **k: None


class AutoDrain:
    _PASS = {"wait_ge", "dma_start", "drain", "sem_inc", "nop", "wait_op"}

    def __init__(self, eng):
        self._e = eng
        self._dirty = []
        self._reads = []

    @staticmethod
    def _span(ap):
        try:
            name = ap.tensor.name
            lo = ap.offset
            dims = list(ap.ap)
            hi = lo + 1
            for step, count in dims[1:]:
                hi += (count - 1) * abs(step)
            return (name, lo, hi)
        except AttributeError:
            return None

    @staticmethod
    def _overlaps(span, lst):
        name, lo, hi = span
        for n, l, h in lst:
            if n == name and lo < h and l < hi:
                return True
        return False

    def __getattr__(self, attr):
        fn = getattr(self._e, attr)
        if attr in AutoDrain._PASS or attr.startswith("_"):
            return fn

        def wrapped(*a, **k):
            outs, ins = [], []
            if "out" in k:
                outs.append(k["out"])
                ins.extend(a)
            elif a:
                outs.append(a[0])
                ins.extend(a[1:])
            if k.get("accum_out") is not None:
                outs.append(k["accum_out"])
            for key, x in k.items():
                if key not in ("out", "accum_out") and x is not None:
                    ins.append(x)
            out_spans = [s for s in (self._span(x) for x in outs) if s]
            in_spans = [s for s in (self._span(x) for x in ins) if s]
            if (any(self._overlaps(s, self._dirty) for s in in_spans + out_spans)
                    or any(self._overlaps(s, self._reads) for s in out_spans)):
                self._e.drain()
                self._dirty = []
                self._reads = []
            inst = fn(*a, **k)
            self._dirty.extend(out_spans)
            self._reads.extend(in_spans)
            return inst

        return wrapped


def build_nc():
    nc = bass.Bass("TRN2", target_bir_lowering=False)

    x_own = nc.dram_tensor("x_own", [HALF, D], F32, kind="ExternalInput").ap()
    x_prev = nc.dram_tensor("x_prev", [HALF, D], F32, kind="ExternalInput").ap()
    wmod_d = nc.dram_tensor("wmod_d", [D, 6 * D], F32, kind="ExternalInput").ap()
    weT_d = nc.dram_tensor("weT_d", [NE, D, D], F32, kind="ExternalInput").ap()
    wgT_d = nc.dram_tensor("wgT_d", [128, 64], F32, kind="ExternalInput").ap()
    be_d = nc.dram_tensor("be_d", [NE, D], F32, kind="ExternalInput").ap()
    cs_d = nc.dram_tensor("cs_d", [128, 8], F32, kind="ExternalInput").ap()
    dab_d = nc.dram_tensor("dab_d", [128, 48], F32, kind="ExternalInput").ap()
    gam_d = nc.dram_tensor("gam_d", [128, 16], F32, kind="ExternalInput").ap()
    omg_d = nc.dram_tensor("omg_d", [128, 8], F32, kind="ExternalInput").ap()
    sgate_d = nc.dram_tensor("sgate_d", [128, 1], F32, kind="ExternalInput").ap()
    out_d = nc.dram_tensor("out_o", [HALF, D], F32, kind="ExternalOutput").ap()

    es = ExitStack()
    sbt = lambda name, shape, dt: es.enter_context(nc.sbuf_tensor(name, shape, dt))
    pst = lambda name, shape, dt: es.enter_context(nc.psum_tensor(name, shape, dt))

    x_sb = sbt("x_sb", [128, NJ, 1024], F32)
    buf_fm = sbt("buf_fm", [128, ND, 2048], BF16)
    web = sbt("web", [128, 2, ND, 1024], BF16)
    wmb = sbt("wmb", [128, 2, ND, 512], BF16)
    t0s = sbt("t0s", [128, 2048], BF16)
    t1s = sbt("t1s", [128, 2048], BF16)
    prevs = sbt("prevs", [128, 2048], BF16)
    h_scr = sbt("h_scr", [128, 2048], BF16)
    res_scr = sbt("res_scr", [128, 2048], BF16)
    xn_scr = sbt("xn_scr", [128, 4, 1024], BF16)
    scr4 = sbt("scr4", [128, 4, 1024], F32)
    stats = sbt("stats", [128, 8, 32], F32)
    gbrow = sbt("gbrow", [1, 1024], F32)
    mstage = gbrow[0:1, 0:512]
    gb_bc = sbt("gb_bc", [128, 1024], BF16)
    rgb_bc = sbt("rgb_bc", [128, 1024], BF16)
    mcols = sbt("mcols", [128, 48], F32)
    tiny = sbt("tiny", [128, 2, 8], F32)
    pq = sbt("pq", [128, 16], F32)
    qq = sbt("qq", [128, 16], F32)
    coef = sbt("coef", [128, 16], F32)
    geo = sbt("geo", [128, 16], F32)
    sxn = sbt("sxn", [128, 16], F32)
    stini = sbt("stini", [128, 16], F32)
    dab = sbt("dab", [128, 48], F32)
    gam = sbt("gam", [128, 16], F32)
    omg = sbt("omg", [128, 8], F32)
    sgate = sbt("sgate", [128, 1], F32)
    cs = sbt("cs", [128, 8], F32)
    csb = sbt("csb", [128, 8], BF16)
    wgT = sbt("wgT", [128, 64], BF16)
    be_bf = sbt("be_bf", [8, 1024], BF16)
    lgs = sbt("lgs", [128, 8], F32)
    m8 = sbt("m8", [128, 8], F32)
    wcol = sbt("wcol", [128, 4], F32)
    cw_all = sbt("cw_all", [128, NJ, 8], F32)
    cwT = sbt("cwT", [8, NJ, 128], BF16)
    ident = sbt("ident", [128, 128], BF16)
    identf = sbt("identf", [128, 128], F32)
    vcst = sbt("vcst", [128, 3], F32)
    ones_r = sbt("ones_r", [1, 128], F32)

    pfn = [pst("pfn0", [128, 1024], F32), pst("pfn1", [128, 1024], F32)]
    ptrs = [pst("ptr", [128, 1024], BF16), pst("ptr2", [128, 1024], BF16),
            pst("ptr3", [128, 1024], BF16)]
    psml = pst("psml", [128, 256], F32)
    pmods = [pfn[0][0:1, 0:512], pfn[1][0:1, 0:512]]
    pcwt = psml[0:8, 128:256]

    sems = {s_: es.enter_context(nc.semaphore(s_)) for s_ in SEM_NAMES}

    seq = []
    for j in range(8):
        seq.append((0, j))
    seq += [("b", j) for j in range(8)]
    for e in range(1, NE):
        seq += [(e, j) for j in range(8)]
    for j in range(8, 16):
        seq.append((7, j))
    seq += [("b", j) for j in range(8, 16)]
    for e in (6, 5, 4, 3, 2, 1, 0):
        seq += [(e, j) for j in range(8, 16)]
    A_LAST = {e: seq.index((e, 7)) for e in range(NE)}
    B_RELOADS = [(e, seq.index((e + 2, 15))) for e in (5, 4, 3, 2, 1, 0)]
    LI = {j: seq.index((7, j)) if j < 8 else seq.index((0, j)) for j in range(NJ)}

    def emit_seq_item(t, em, i):
        it = seq[i]
        g_ = pfn[i % 2]
        if i >= 2:
            em.wait(t, "sv", f"v_grp{i - 2}")
        if it[0] == "b":
            j = it[1]
            em.wait(t, "sc", f"c_cwT{j}")
            if i == 0:
                em.wait(t, "sc", "c_gbbc")
            if i == 1:
                em.wait(t, "sc", "c_rgbbc")
            t.matmul(g_[:, 0:512], cwT[0:8, j, :], be_bf[0:8, 0:512],
                     start=True, stop=True)
            inst = t.matmul(g_[:, 512:1024], cwT[0:8, j, :], be_bf[0:8, 512:1024],
                            start=True, stop=True)
        else:
            e, j = it
            if j == 0:
                em.waitv(t, "swe%d" % e, 16)
            elif j == 8 and e <= 5:
                em.waitv(t, "swe%d" % e, 32)
            for dk in range(ND):
                for hh in range(2):
                    inst = t.matmul(g_[:, 512 * hh:512 * hh + 512],
                                    buf_fm[:, dk, 128 * j:128 * j + 128],
                                    web[:, e % 2, dk, 512 * hh:512 * hh + 512],
                                    start=(dk == 0), stop=(dk == ND - 1))
        em.inc(inst, "st", 1, f"t_grp{i}")

    def stream_gpsimd(g, em):
        for src_ap, dst in [(cs_d, cs), (dab_d, dab), (gam_d, gam), (omg_d, omg),
                            (sgate_d, sgate), (wgT_d, wgT)]:
            inst = g.dma_start(out=dst[:], in_=src_ap)
            em.inc(inst, "sdsm", 16)
        inst = g.dma_start(out=be_bf[:], in_=be_d)
        em.inc(inst, "sdsm", 16, "w_small")
        g.memset(ident[:], 1.0)
        g.memset(ones_r[:], 1.0)
        inst = g.memset(identf[:], 1.0)
        em.inc(inst, "sg", 1, "g_memset")
        em.wait(g, "sg", "g_memset")
        g.affine_select(ident[:], ident[:], [[-1, 128]], ALU.is_equal, 0.0,
                        channel_multiplier=1)
        inst = g.affine_select(identf[:], identf[:], [[-1, 128]], ALU.is_equal, 0.0,
                               channel_multiplier=1)
        em.inc(inst, "sg", 1, "g_ident")
        wm_r = wmod_d.rearrange("(k p) m -> p k m", p=128)
        for cb in range(12):
            if cb >= 2:
                em.wait(g, "st", f"t_gemv{cb - 2}")
            inst = g.dma_start(out=wmb[:, cb % 2], in_=wm_r[:, :, 512 * cb:512 * cb + 512])
            em.inc(inst, ["sdwm", "sdwm1"][cb % 2], 16, f"w_wmod{cb}")
        for e in range(2):
            inst = g.dma_start(out=web[:, e % 2],
                               in_=weT_d[e].rearrange("(k p) o -> p k o", p=128))
            em.inc(inst, "swe%d" % e, 16, f"w_we{e}")
        em.wait(g, "sc", "c_rgbbc")
        for j in range(8):
            em.wait(g, "sc", f"c_xn2_{j}")
            inst = g.tensor_tensor(x_sb[:, j, :], x_sb[:, j, :], rgb_bc[:], ALU.mult)
            em.inc(inst, "sg", 1, f"g_acci{j}")
        for e in range(2, NE):
            em.wait(g, "st", f"t_grp{A_LAST[e - 2]}")
            inst = g.dma_start(out=web[:, e % 2],
                               in_=weT_d[e].rearrange("(k p) o -> p k o", p=128))
            em.inc(inst, "swe%d" % e, 16, f"w_we{e}")
        for j in range(8, NJ):
            em.wait(g, "sc", f"c_xn2_{j}")
            inst = g.tensor_tensor(x_sb[:, j, :], x_sb[:, j, :], rgb_bc[:], ALU.mult)
            em.inc(inst, "sg", 1, f"g_acci{j}")
        for j in range(8):
            em.wait(g, "sv", f"v_grp{LI[j]}")
            if j >= 4:
                em.waitv(g, ["sdo", "sdo1", "sdo2", "sdo3"][j % 4], 16 * (j // 4))
            inst = g.tensor_tensor(scr4[:, j % 4, :], x_sb[:, j, :], gb_bc[:], ALU.mult)
            em.inc(inst, "sg", 1, f"g_fin{j}")
        for e, pred in B_RELOADS:
            em.wait(g, "st", f"t_grp{pred}")
            inst = g.dma_start(out=web[:, e % 2],
                               in_=weT_d[e].rearrange("(k p) o -> p k o", p=128))
            em.inc(inst, "swe%d" % e, 16, f"w_we{e}b")
        for j in range(8, NJ):
            em.wait(g, "sv", f"v_grp{LI[j]}")
            em.waitv(g, ["sdo", "sdo1", "sdo2", "sdo3"][j % 4], 16 * (j // 4))
            inst = g.tensor_tensor(scr4[:, j % 4, :], x_sb[:, j, :], gb_bc[:], ALU.mult)
            em.inc(inst, "sg", 1, f"g_fin{j}")

    def stream_sync(s, em):
        for j in range(4):
            inst = s.dma_start(out=scr4[:, j % 4, :], in_=x_prev[128 * j:128 * j + 128, :])
            em.inc(inst, ["sdxp", "sdxp1", "sdxp2", "sdxp3"][j % 4], 16, f"xp{j}")
        inst = s.dma_start(out=x_sb[:], in_=x_own.rearrange("(c p) d -> p c d", p=128))
        em.inc(inst, "sdx", 16, "x_own")
        for j in range(4, NJ):
            em.wait(s, "sc", f"c_xnp{j - 4}")
            inst = s.dma_start(out=scr4[:, j % 4, :], in_=x_prev[128 * j:128 * j + 128, :])
            em.inc(inst, ["sdxp", "sdxp1", "sdxp2", "sdxp3"][j % 4], 16, f"xp{j}")
        for j in range(NJ):
            em.wait(s, "sg", f"g_fin{j}")
            inst = s.dma_start(out=out_d[128 * j:128 * j + 128, :], in_=scr4[:, j % 4, :])
            em.inc(inst, ["sdo", "sdo1", "sdo2", "sdo3"][j % 4], 16, f"o{j}")
        for r in ("sdo", "sdo1", "sdo2", "sdo3"):
            em.waitv(s, r, 16 * 4)

    def stream_scalar(c, em):
        em.wait(c, "sdsm", "w_small")
        inst = c.activation(out=csb[:], in_=cs[:], func=AF.Silu)
        em.inc(inst, "sc", 1, "c_silu_cs")
        c.activation(out=pq[:], in_=dab[:, 0:16], func=AF.Sigmoid)
        inst = c.activation(out=geo[:], in_=dab[:, 16:32], func=AF.Sigmoid)
        em.inc(inst, "sc", 1, "c_sigs")
        em.wait(c, "sv", "v_qq")
        inst = c.activation(out=geo[:], in_=qq[:], func=AF.Ln)
        em.inc(inst, "sc", 1, "c_lnq")
        em.wait(c, "sv", "v_coef")
        inst = c.activation(out=pq[:], in_=geo[:], func=AF.Exp, scale=vcst[:, 1:2])
        em.inc(inst, "sc", 1, "c_q2048")

        for j in range(NJ):
            em.wait(c, "sv", f"v_pstat{j}")
            c.activation(out=stats[:, 7, j:j + 1], in_=stats[:, 0, j:j + 1],
                         func=AF.Square, scale=RD)
            c.mul(out=stats[:, 7, j:j + 1], in_=stats[:, 7, j:j + 1], mul=-1.0)
            c.activation(out=stats[:, 3, j:j + 1], in_=stats[:, 1, j:j + 1],
                         func=AF.Identity, scale=RD)
            c.add(out=stats[:, 3, j:j + 1], in_=stats[:, 3, j:j + 1], add=stats[:, 7, j:j + 1])
            c.activation(out=stats[:, 4, j:j + 1], in_=stats[:, 3, j:j + 1],
                         func=AF.Ln, bias=vcst[:, 0:1])
            c.activation(out=stats[:, 5, j:j + 1], in_=stats[:, 4, j:j + 1],
                         func=AF.Exp, scale=vcst[:, 2:3])
            c.activation(out=stats[:, 6, j:j + 1], in_=stats[:, 0, j:j + 1],
                         func=AF.Identity, scale=stats[:, 5, j:j + 1])
            c.mul(out=stats[:, 6, j:j + 1], in_=stats[:, 6, j:j + 1], mul=-RD)
            inst = c.activation(out=xn_scr[:, j % 4, :], in_=scr4[:, j % 4, :], func=AF.Identity,
                                scale=stats[:, 5, j:j + 1], bias=stats[:, 6, j:j + 1])
            em.inc(inst, "sc", 1, f"c_xnp{j}")
            em.wait(c, "st", f"t_trp{j}")
            inst = c.copy(out=buf_fm[:, :, 128 * j:128 * j + 128],
                          in_=ptrs[j % 3].ap().rearrange("p (k t) -> p k t", k=ND))
            em.inc(inst, "sc", 1, f"c_evp{j}")
            if 2 <= j < 14:
                cb = j - 2
                em.wait(c, "st", f"t_gemv{cb}")
                if cb >= 2:
                    em.wait(c, "st", f"t_k1_{cb - 2}")
                inst = c.copy(out=gbrow[0:1, 512 * (cb % 2):512 * (cb % 2) + 512],
                              in_=pmods[cb % 2])
                em.inc(inst, "sc", 1, f"c_gemv{cb}")
        em.wait(c, "st", "t_k1_9")
        inst = c.copy(out=mcols[:, 0:40], in_=psml[:, 0:40])
        em.inc(inst, "sc", 1, "c_mcols")

        em.wait(c, "sv", "v_stato")
        c.activation(out=stats[:, 4, 16:32], in_=stats[:, 3, 16:32],
                     func=AF.Ln, bias=vcst[:, 0:1])
        c.activation(out=stats[:, 5, 16:32], in_=stats[:, 4, 16:32],
                     func=AF.Exp, scale=vcst[:, 2:3])
        for j in range(NJ):
            c.activation(out=stats[:, 6, 16 + j:17 + j], in_=stats[:, 2, 16 + j:17 + j],
                         func=AF.Identity, scale=stats[:, 5, 16 + j:17 + j])
            c.mul(out=stats[:, 6, 16 + j:17 + j], in_=stats[:, 6, 16 + j:17 + j], mul=-1.0)
        for j in range(NJ):
            if j >= 4:
                em.wait(c, "st", f"t_tro{j - 4}")
            inst = c.activation(out=xn_scr[:, j % 4, :], in_=x_sb[:, j, :], func=AF.Identity,
                                scale=stats[:, 5, 16 + j:17 + j], bias=stats[:, 6, 16 + j:17 + j])
            em.inc(inst, "sc", 1, f"c_xno{j}")
            if j >= 3:
                em.wait(c, "st", f"t_tro{j - 3}")
                if j == 3:
                    em.wait(c, "sv", "v_sxn_all")
                inst = c.copy(out=buf_fm[:, :, 128 * (j - 3):128 * (j - 3) + 128],
                              in_=ptrs[(j - 3) % 3].ap().rearrange("p (k t) -> p k t", k=ND))
                em.inc(inst, "sc", 1, f"c_evo{j - 3}")
        for jj in (NJ - 3, NJ - 2, NJ - 1):
            em.wait(c, "st", f"t_tro{jj}")
            inst = c.copy(out=buf_fm[:, :, 128 * jj:128 * jj + 128],
                          in_=ptrs[jj % 3].ap().rearrange("p (k t) -> p k t", k=ND))
            em.inc(inst, "sc", 1, f"c_evo{jj}")
        em.wait(c, "st", "t_gbbc")
        inst = c.copy(out=gb_bc[:], in_=pfn[0][:])
        em.inc(inst, "sc", 1, "c_gbbc")
        em.wait(c, "st", "t_rgbbc")
        inst = c.copy(out=rgb_bc[:], in_=pfn[1][:])
        em.inc(inst, "sc", 1, "c_rgbbc")

        for dk in range(ND):
            em.wait(c, "sv", f"v_pre{dk}")
            if dk >= 1:
                em.wait(c, "sv", f"v_dfm{dk - 1}")
            inst = c.activation(out=res_scr[:], in_=prevs[:], func=AF.Silu)
            em.inc(inst, "sc", 1, f"c_sil{dk}")

        for j in range(NJ):
            em.wait(c, "sv", f"v_x2_{j}")
            inst = c.activation(out=scr4[:, j % 2, :], in_=x_sb[:, j, :], func=AF.Square,
                                accum_out=stats[:, 1, 16 + j:17 + j])
            if j == NJ - 1:
                em.inc(inst, "sc", 1, "c_sq2")
        em.wait(c, "sv", "v_stat2")
        c.activation(out=stats[:, 4, 16:32], in_=stats[:, 3, 16:32],
                     func=AF.Ln, bias=vcst[:, 0:1])
        c.activation(out=stats[:, 5, 16:32], in_=stats[:, 4, 16:32],
                     func=AF.Exp, scale=vcst[:, 2:3])
        for j in range(NJ):
            c.activation(out=stats[:, 6, 16 + j:17 + j], in_=stats[:, 2, 16 + j:17 + j],
                         func=AF.Identity, scale=stats[:, 5, 16 + j:17 + j])
            c.mul(out=stats[:, 6, 16 + j:17 + j], in_=stats[:, 6, 16 + j:17 + j], mul=-1.0)

        def _h2_chunk(jj):
            em.wait(c, "st", f"t_trx{jj}")
            for dk in range(ND):
                inst = c.activation(out=buf_fm[:, dk, 128 * jj:128 * jj + 128],
                                    in_=ptrs[jj % 3][:, 128 * dk:128 * dk + 128], func=AF.Identity,
                                    scale=mcols[:, 32 + dk:33 + dk],
                                    bias=mcols[:, 24 + dk:25 + dk])
            em.inc(inst, "sc", 1, f"c_evh{jj}")
            if jj >= 2:
                em.wait(c, "st", f"t_cwT{jj - 2}")
                inst = c.copy(out=cwT[0:8, jj - 2, :], in_=pcwt)
                em.inc(inst, "sc", 1, f"c_cwT{jj - 2}")
            em.wait(c, "st", f"t_lg{jj}")
            if jj >= 1:
                em.wait(c, "sv", f"v_cw{jj - 1}")
            inst = c.copy(out=lgs[:], in_=psml[:, 48:56])
            em.inc(inst, "sc", 1, f"c_lg{jj}")
            em.wait(c, "sv", f"v_d21_{jj}")
            inst = c.activation(out=wcol[:, 1:2], in_=wcol[:, 0:1], func=AF.Exp)
            em.inc(inst, "sc", 1, f"c_exp{jj}")

        for j in range(NJ):
            if j >= 4:
                em.wait(c, "st", f"t_trx{j - 4}")
            inst = c.activation(out=xn_scr[:, j % 4, :], in_=x_sb[:, j, :], func=AF.Identity,
                                scale=stats[:, 5, 16 + j:17 + j], bias=stats[:, 6, 16 + j:17 + j])
            em.inc(inst, "sc", 1, f"c_xn2_{j}")
            if j >= 1:
                _h2_chunk(j - 1)
        _h2_chunk(NJ - 1)
        for jj in (NJ - 2, NJ - 1):
            em.wait(c, "st", f"t_cwT{jj}")
            inst = c.copy(out=cwT[0:8, jj, :], in_=pcwt)
            em.inc(inst, "sc", 1, f"c_cwT{jj}")

    def stream_tensor(t, em):
        em.wait(t, "sg", "g_ident")
        em.wait(t, "sc", "c_silu_cs")
        for j in range(NJ):
            em.wait(t, "sc", f"c_xnp{j}")
            if j >= 3:
                em.wait(t, "sc", f"c_evp{j - 3}")
            for dk in range(ND):
                inst = t.transpose(ptrs[j % 3][:, 128 * dk:128 * dk + 128],
                                   xn_scr[:, j % 4, 128 * dk:128 * dk + 128], ident[:])
            em.inc(inst, "st", 1, f"t_trp{j}")
            if 3 <= j <= 12:
                pb = j - 3
                base = 512 * (pb % 2)
                for i in range(4):
                    m = 4 * pb + i
                    inst = t.matmul(psml[:, m:m + 1],
                                    gbrow[0:1, base + 128 * i:base + 128 * i + 128],
                                    ones_r[0:1, 0:1], start=True, stop=True)
                em.inc(inst, "st", 1, f"t_k1_{pb}")
            if j < 12:
                cb = j
                em.waitv(t, ["sdwm", "sdwm1"][cb % 2], 16 * (cb // 2 + 1))
                if cb >= 2:
                    em.wait(t, "sc", f"c_gemv{cb - 2}")
                for dk in range(ND):
                    inst = t.matmul(pmods[cb % 2], csb[:, dk:dk + 1], wmb[:, cb % 2, dk, :],
                                    start=(dk == 0), stop=(dk == ND - 1))
                em.inc(inst, "st", 1, f"t_gemv{cb}")
        em.wait(t, "sv", "v_gb1")
        t.matmul(pfn[0][:, 0:512], ones_r[0:1, :], gbrow[0:1, 0:512], start=True, stop=True)
        inst = t.matmul(pfn[0][:, 512:1024], ones_r[0:1, :], gbrow[0:1, 512:1024],
                        start=True, stop=True)
        em.inc(inst, "st", 1, "t_gbbc")
        em.wait(t, "sv", "v_rgb2")
        t.matmul(pfn[1][:, 0:512], ones_r[0:1, :], gbrow[0:1, 0:512], start=True, stop=True)
        inst = t.matmul(pfn[1][:, 512:1024], ones_r[0:1, :], gbrow[0:1, 512:1024],
                        start=True, stop=True)
        em.inc(inst, "st", 1, "t_rgbbc")
        for j in range(NJ):
            em.wait(t, "sc", f"c_xno{j}")
            em.wait(t, "sc", f"c_evp{13 + j}" if j < 3 else f"c_evo{j - 3}")
            for dk in range(ND):
                inst = t.transpose(ptrs[j % 3][:, 128 * dk:128 * dk + 128],
                                   xn_scr[:, j % 4, 128 * dk:128 * dk + 128], ident[:])
            em.inc(inst, "st", 1, f"t_tro{j}")
        for j in range(NJ):
            if j == 0:
                em.wait(t, "sv", "v_dfm7")
            elif j >= 3:
                em.wait(t, "sv", f"v_x2_{j - 3}")
            for dk in range(ND):
                inst = t.transpose(ptrs[j % 3][:, 128 * dk:128 * dk + 128],
                                   buf_fm[:, dk, 128 * j:128 * j + 128], ident[:])
            em.inc(inst, "st", 1, f"t_trd{j}")

        def _logits_chunk(jj):
            em.wait(t, "sc", f"c_evh{jj}")
            em.wait(t, "sc", "c_mcols" if jj == 0 else f"c_lg{jj - 1}")
            if jj >= 2:
                em.wait(t, "sc", f"c_cwT{jj - 2}")
            for dk in range(ND):
                inst = t.matmul(psml[:, 48:56], buf_fm[:, dk, 128 * jj:128 * jj + 128],
                                wgT[:, 8 * dk:8 * dk + 8],
                                start=(dk == 0), stop=(dk == ND - 1))
            em.inc(inst, "st", 1, f"t_lg{jj}")

        def _cwt_chunk(jj):
            em.wait(t, "sv", f"v_cw{jj}")
            if jj >= 1:
                em.wait(t, "sc", f"c_cwT{jj - 1}")
            inst = t.transpose(pcwt, cw_all[:, jj, :], identf[:])
            em.inc(inst, "st", 1, f"t_cwT{jj}")

        sq_i = 0
        for j in range(NJ):
            em.wait(t, "sc", f"c_xn2_{j}")
            if j < 3:
                em.wait(t, "sv", f"v_x2_{[15, 13, 14][j]}")
            if j >= 3:
                em.wait(t, "sc", f"c_evh{j - 3}")
            for dk in range(ND):
                inst = t.transpose(ptrs[j % 3][:, 128 * dk:128 * dk + 128],
                                   xn_scr[:, j % 4, 128 * dk:128 * dk + 128], ident[:])
            em.inc(inst, "st", 1, f"t_trx{j}")
            if j >= 1:
                _logits_chunk(j - 1)
            if j >= 2:
                _cwt_chunk(j - 2)
            emitted = 0
            while sq_i < len(seq) and seq[sq_i][1] <= j - 2 and emitted < 8:
                emit_seq_item(t, em, sq_i)
                sq_i += 1
                emitted += 1
        _logits_chunk(NJ - 1)
        _cwt_chunk(NJ - 2)
        _cwt_chunk(NJ - 1)
        while sq_i < len(seq):
            emit_seq_item(t, em, sq_i)
            sq_i += 1

    def stream_vector(v, em):
        v.memset(vcst[:, 0:1], EPS)
        v.memset(vcst[:, 1:2], 2048.0)
        v.memset(vcst[:, 2:3], -0.5)
        em.wait(v, "sc", "c_sigs")
        v.tensor_tensor(stini[:], pq[:], geo[:], ALU.mult)
        inst = v.tensor_scalar(qq[:], stini[:], -1.0, 1.0, ALU.mult, ALU.add)
        em.inc(inst, "sv", 1, "v_qq")
        v.tensor_tensor(sxn[:], pq[:], dab[:, 32:48], ALU.mult)
        v.tensor_tensor(coef[:], sxn[:], gam[:], ALU.mult)
        inst = v.tensor_scalar(coef[:], coef[:], SCALE, None, ALU.mult)
        em.inc(inst, "sv", 1, "v_coef")
        em.wait(v, "sc", "c_q2048")
        v.tensor_scalar(pq[:], pq[:], -1.0, 1.0, ALU.mult, ALU.add)
        v.tensor_scalar(stini[:], qq[:], -1.0, 1.0, ALU.mult, ALU.add)
        v.reciprocal(stini[:], stini[:])
        inst = v.tensor_tensor(geo[:], pq[:], stini[:], ALU.mult)
        em.inc(inst, "sv", 1, "v_geo")

        dumps = [prevs[:, 0:1024], prevs[:, 1024:2048], t0s[:, 0:1024], t0s[:, 1024:2048],
                 t1s[:, 0:1024], t1s[:, 1024:2048],
                 h_scr[:, 0:1024], h_scr[:, 1024:2048], res_scr[:, 0:1024], res_scr[:, 1024:2048]]
        em.wait(v, "sdx", "x_own")
        for j in range(NJ):
            v.reduce_sum(out=stats[:, 0, 16 + j:17 + j], in_=x_sb[:, j, :],
                         axis=mybir.AxisListType.X)
            v.scalar_tensor_tensor(dumps[j % 10], x_sb[:, j, :], 1.0, x_sb[:, j, :],
                                   ALU.mult, ALU.mult, accum_out=stats[:, 1, 16 + j:17 + j])
        v.tensor_scalar(stats[:, 2, 16:32], stats[:, 0, 16:32], RD, None, ALU.mult)
        v.tensor_scalar(stats[:, 4, 16:32], stats[:, 1, 16:32], RD, None, ALU.mult)
        v.tensor_tensor(stats[:, 7, 16:32], stats[:, 2, 16:32], stats[:, 2, 16:32], ALU.mult)
        inst = v.tensor_tensor(stats[:, 3, 16:32], stats[:, 4, 16:32], stats[:, 7, 16:32],
                               ALU.subtract)
        em.inc(inst, "sv", 1, "v_stato")

        for j in range(NJ):
            em.waitv(v, ["sdxp", "sdxp1", "sdxp2", "sdxp3"][j % 4], 16 * (j // 4 + 1))
            v.scalar_tensor_tensor(dumps[j % 10], scr4[:, j % 4, :], 1.0,
                                   scr4[:, j % 4, :], ALU.mult, ALU.mult,
                                   accum_out=stats[:, 1, j:j + 1])
            inst = v.reduce_sum(out=stats[:, 0, j:j + 1], in_=scr4[:, j % 4, :],
                                axis=mybir.AxisListType.X)
            em.inc(inst, "sv", 1, f"v_pstat{j}")

        em.wait(v, "sc", "c_mcols")
        v.tensor_scalar(mcols[:, 8:16], mcols[:, 8:16], 1.0, None, ALU.add)
        v.tensor_scalar(mcols[:, 32:40], mcols[:, 32:40], 1.0, None, ALU.add)
        v.tensor_scalar(tiny[:, 0, :], mcols[:, 16:24], 0.0, None, ALU.is_ge)
        v.tensor_scalar(tiny[:, 0, :], tiny[:, 0, :], 2.0, -1.0, ALU.mult, ALU.add)
        v.tensor_scalar(tiny[:, 1, :], mcols[:, 16:24], -1.0, None, ALU.mult)
        v.tensor_tensor(tiny[:, 1, :], tiny[:, 1, :], mcols[:, 16:24], ALU.max)
        v.tensor_scalar(tiny[:, 1, :], tiny[:, 1, :], 0.25, None, ALU.max)
        inst = v.tensor_tensor(mcols[:, 16:24], tiny[:, 0, :], tiny[:, 1, :], ALU.mult)
        em.inc(inst, "sv", 1, "v_ca1")
        em.wait(v, "sc", "c_gemv11")
        v.tensor_scalar(rgb_bc[0:1, :], gbrow[0:1, :], 0.0, None, ALU.is_ge)
        v.tensor_scalar(rgb_bc[0:1, :], rgb_bc[0:1, :], 2.0, -1.0, ALU.mult, ALU.add)
        v.tensor_scalar(gb_bc[0:1, :], gbrow[0:1, :], -1.0, None, ALU.mult)
        v.tensor_tensor(gb_bc[0:1, :], gb_bc[0:1, :], gbrow[0:1, :], ALU.max)
        v.tensor_scalar(gb_bc[0:1, :], gb_bc[0:1, :], 0.25, None, ALU.max)
        inst = v.tensor_tensor(gbrow[0:1, :], rgb_bc[0:1, :], gb_bc[0:1, :], ALU.mult)
        em.inc(inst, "sv", 1, "v_gb1")

        em.wait(v, "st", "t_gbbc")
        inst = v.reciprocal(gbrow[0:1, :], gbrow[0:1, :])
        em.inc(inst, "sv", 1, "v_rgb2")

        em.wait(v, "sc", f"c_evp{NJ - 1}")
        for dk in range(ND):
            for n in range(2):
                v.tensor_tensor_scan(out=prevs[:], data0=qq[:, 2 * dk + n:2 * dk + n + 1].to_broadcast([128, 2048]),
                                     data1=buf_fm[:, dk, :], initial=0.0,
                                     op0=ALU.mult, op1=ALU.add)
                inst = v.tensor_copy(sxn[:, 2 * dk + n:2 * dk + n + 1], prevs[:, 2047:2048])
        em.inc(inst, "sv", 1, "v_vsc")

        for dk in range(ND):
            sl = slice(2 * dk, 2 * dk + 2)
            v.tensor_scalar(stini[:, sl], sxn[:, sl], mcols[:, 8 + dk:9 + dk], None, ALU.mult)
            v.scalar_tensor_tensor(stini[:, sl], geo[:, sl], mcols[:, dk:dk + 1],
                                   stini[:, sl], ALU.mult, ALU.add)
            inst = v.tensor_scalar(stini[:, sl], stini[:, sl], sgate[:, 0:1], None, ALU.mult)
        em.inc(inst, "sv", 1, "v_sxn_all")
        em.wait(v, "sc", f"c_evo{NJ - 1}")
        for dk in range(ND):
            v.tensor_scalar(h_scr[:], buf_fm[:, dk, :], mcols[:, 8 + dk:9 + dk],
                            mcols[:, dk:dk + 1], ALU.mult, ALU.add)
            v.tensor_tensor_scan(out=t0s[:], data0=qq[:, 2 * dk:2 * dk + 1].to_broadcast([128, 2048]),
                                 data1=h_scr[:], initial=stini[:, 2 * dk:2 * dk + 1],
                                 op0=ALU.mult, op1=ALU.add)
            v.tensor_tensor_scan(out=t1s[:], data0=qq[:, 2 * dk + 1:2 * dk + 2].to_broadcast([128, 2048]),
                                 data1=h_scr[:], initial=stini[:, 2 * dk + 1:2 * dk + 2],
                                 op0=ALU.mult, op1=ALU.add)
            if dk >= 1:
                em.wait(v, "sc", f"c_sil{dk - 1}")
                inst = v.tensor_scalar(buf_fm[:, dk - 1, :], res_scr[:],
                                       mcols[:, 16 + dk - 1:17 + dk - 1], None, ALU.mult)
                em.inc(inst, "sv", 1, f"v_dfm{dk - 1}")
            v.tensor_scalar(res_scr[:], h_scr[:], omg[:, dk:dk + 1], None, ALU.mult)
            v.scalar_tensor_tensor(t0s[:], t0s[:], coef[:, 2 * dk:2 * dk + 1], res_scr[:],
                                   ALU.mult, ALU.add)
            inst = v.scalar_tensor_tensor(prevs[:], t1s[:], coef[:, 2 * dk + 1:2 * dk + 2],
                                          t0s[:], ALU.mult, ALU.add)
            em.inc(inst, "sv", 1, f"v_pre{dk}")
        em.wait(v, "sc", f"c_sil{ND - 1}")
        inst = v.tensor_scalar(buf_fm[:, ND - 1, :], res_scr[:],
                               mcols[:, 16 + ND - 1:17 + ND - 1], None, ALU.mult)
        em.inc(inst, "sv", 1, f"v_dfm{ND - 1}")

        for j in range(NJ):
            em.wait(v, "st", f"t_trd{j}")
            inst = v.tensor_tensor(x_sb[:, j, :], x_sb[:, j, :], ptrs[j % 3][:], ALU.add)
            em.inc(inst, "sv", 1, f"v_x2_{j}")
            v.reduce_sum(out=stats[:, 0, 16 + j:17 + j], in_=x_sb[:, j, :],
                         axis=mybir.AxisListType.X)
        em.wait(v, "sc", "c_sq2")
        v.tensor_scalar(stats[:, 2, 16:32], stats[:, 0, 16:32], RD, None, ALU.mult)
        v.tensor_scalar(stats[:, 4, 16:32], stats[:, 1, 16:32], RD, None, ALU.mult)
        v.tensor_tensor(stats[:, 7, 16:32], stats[:, 2, 16:32], stats[:, 2, 16:32], ALU.mult)
        inst = v.tensor_tensor(stats[:, 3, 16:32], stats[:, 4, 16:32], stats[:, 7, 16:32],
                               ALU.subtract)
        em.inc(inst, "sv", 1, "v_stat2")

        def _comb_item(i):
            it = seq[i]
            g_ = pfn[i % 2]
            em.wait(v, "st", f"t_grp{i}")
            if it[0] == "b":
                jj = it[1]
                em.wait(v, "sg", f"g_acci{jj}")
                inst = v.scalar_tensor_tensor(x_sb[:, jj, :], g_[:], 1.0, x_sb[:, jj, :],
                                              ALU.mult, ALU.add)
            else:
                e, jj = it
                em.wait(v, "sg", f"g_acci{jj}")
                inst = v.scalar_tensor_tensor(x_sb[:, jj, :], g_[:],
                                              cw_all[:, jj, e:e + 1], x_sb[:, jj, :],
                                              ALU.mult, ALU.add)
            em.inc(inst, "sv", 1, f"v_grp{i}")

        cb_i = 0
        for j in range(NJ):
            emitted = 0
            while cb_i < len(seq) and seq[cb_i][1] <= j - 2 and emitted < 8:
                _comb_item(cb_i)
                cb_i += 1
                emitted += 1
            em.wait(v, "sc", f"c_lg{j}")
            v.max(out=m8[:], in_=lgs[:])
            inst = v.tensor_tensor(wcol[:, 0:1], m8[:, 1:2], m8[:, 0:1], ALU.subtract)
            em.inc(inst, "sv", 1, f"v_d21_{j}")
            em.wait(v, "sc", f"c_exp{j}")
            v.tensor_scalar(wcol[:, 2:3], wcol[:, 1:2], 1.0, None, ALU.add)
            v.reciprocal(wcol[:, 2:3], wcol[:, 2:3])
            v.tensor_tensor(wcol[:, 3:4], wcol[:, 1:2], wcol[:, 2:3], ALU.mult)
            v.tensor_scalar(cw_all[:, j, :], lgs[:], m8[:, 0:1], None, ALU.is_equal)
            v.tensor_scalar(cw_all[:, j, :], cw_all[:, j, :], wcol[:, 2:3], None, ALU.mult)
            v.tensor_scalar(tiny[:, 0, :], lgs[:], m8[:, 1:2], None, ALU.is_equal)
            inst = v.scalar_tensor_tensor(cw_all[:, j, :], tiny[:, 0, :], wcol[:, 3:4],
                                          cw_all[:, j, :], ALU.mult, ALU.add)
            em.inc(inst, "sv", 1, f"v_cw{j}")

        while cb_i < len(seq):
            _comb_item(cb_i)
            cb_i += 1

    streams = [stream_gpsimd, stream_sync, stream_scalar, stream_tensor, stream_vector]
    plan = Plan()
    for fn in streams:
        fn(DummyEng(), plan)
    em = Emit(plan, sems)

    with nc.Block() as block:
        @block.gpsimd
        def _(g):
            stream_gpsimd(AutoDrain(g), em)

        @block.sync
        def _(s):
            stream_sync(s, em)

        @block.scalar
        def _(c):
            stream_scalar(AutoDrain(c), em)

        @block.tensor
        def _(t):
            stream_tensor(t, em)

        @block.vector
        def _(v):
            stream_vector(AutoDrain(v), em)

    es.close()
    return nc


_NC = None


def _get_nc():
    global _NC
    if _NC is None:
        _NC = build_nc()
    return _NC


def _pack_cols(a):
    k = a.shape[1]
    return np.ascontiguousarray(a.reshape(8, 128, k).transpose(1, 0, 2).reshape(128, 8 * k))


def kernel(L0_attn_x, L0_attn_delta, L0_attn_alpha, L0_attn_beta, L0_attn_gamma,
           L0_attn_omega, L0_ffn_Wg, L0_ffn_We, L0_ffn_be, L0_c, L0_wmod):
    x = np.ascontiguousarray(np.asarray(L0_attn_x, dtype=np.float32))
    delta = np.asarray(L0_attn_delta, dtype=np.float32).reshape(D, 2)
    alpha = np.asarray(L0_attn_alpha, dtype=np.float32).reshape(D, 2)
    beta = np.asarray(L0_attn_beta, dtype=np.float32).reshape(D, 2)
    gamma = np.asarray(L0_attn_gamma, dtype=np.float32).reshape(D, 2)
    omega = np.asarray(L0_attn_omega, dtype=np.float32).reshape(D, 1)
    Wg = np.asarray(L0_ffn_Wg, dtype=np.float32)
    We = np.asarray(L0_ffn_We, dtype=np.float32)
    be = np.ascontiguousarray(np.asarray(L0_ffn_be, dtype=np.float32))
    cc = np.asarray(L0_c, dtype=np.float32)
    wmod = np.ascontiguousarray(np.asarray(L0_wmod, dtype=np.float32))

    weT = np.ascontiguousarray(We.transpose(0, 2, 1))
    wgT = np.ascontiguousarray(Wg.T.reshape(8, 128, 8).transpose(1, 0, 2).reshape(128, 64))
    dab = np.ascontiguousarray(np.concatenate(
        [_pack_cols(delta), _pack_cols(alpha), _pack_cols(beta)], axis=1))
    gam = _pack_cols(gamma)
    omg = _pack_cols(omega)

    in_maps = []
    for c_id in range(8):
        b, h = c_id // 2, c_id % 2
        in_maps.append({
            "x_own": np.ascontiguousarray(x[b, HALF * h:HALF * (h + 1), :]),
            "x_prev": np.ascontiguousarray(x[b, 0:HALF, :]),
            "wmod_d": wmod,
            "weT_d": weT,
            "wgT_d": wgT,
            "be_d": be,
            "cs_d": np.ascontiguousarray(cc[b].reshape(8, 128).T),
            "dab_d": dab,
            "gam_d": gam,
            "omg_d": omg,
            "sgate_d": np.full((128, 1), float(h), dtype=np.float32),
        })

    nc = _get_nc()
    res = run_bass_kernel_spmd(nc, in_maps, core_ids=list(range(8)))
    out = np.empty((B, S, D), dtype=np.float32)
    for c_id in range(8):
        b, h = c_id // 2, c_id % 2
        out[b, HALF * h:HALF * (h + 1), :] = res.results[c_id]["out_o"]
    return out


# version 27
# speedup vs baseline: 1.0429x; 1.0429x over previous
import math
from contextlib import ExitStack

import numpy as np
import concourse.bass as bass
import concourse.mybir as mybir
from concourse.bass_utils import run_bass_kernel_spmd

F32 = mybir.dt.float32
BF16 = mybir.dt.bfloat16
AF = mybir.ActivationFunctionType
ALU = mybir.AluOpType

B, S, D = 4, 4096, 1024
HALF = 2048
NJ = 16
ND = 8
NE = 8
SCALE = math.sqrt(0.5)
EPS = 1e-5
RD = 1.0 / D

SEM_NAMES = (["sdx", "sdxp", "sdxp1", "sdxp2", "sdxp3", "sdsm", "sdwm", "sdwm1", "sdo", "sdo1", "sdo2", "sdo3", "sg", "sc", "st", "sv"]
             + ["swe%d" % e for e in range(8)])


class Plan:
    emitting = False

    def __init__(self):
        self.ctr = dict.fromkeys(SEM_NAMES, 0)
        self.val = {}

    def inc(self, inst, sem, n, key=None):
        self.ctr[sem] += n
        if key is not None:
            assert key not in self.val, f"dup key {key}"
            self.val[key] = self.ctr[sem]

    def wait(self, eng, sem, key):
        pass

    def waitv(self, eng, sem, v):
        pass


class Emit:
    emitting = True

    def __init__(self, plan, sems):
        self.val = plan.val
        self.sems = sems

    def inc(self, inst, sem, n, key=None):
        inst.then_inc(self.sems[sem], n)

    def wait(self, eng, sem, key):
        eng.wait_ge(self.sems[sem], self.val[key])

    def waitv(self, eng, sem, v):
        eng.wait_ge(self.sems[sem], v)


class DummyEng:
    def __getattr__(self, name):
        return lambda *a, **k: None


class AutoDrain:
    _PASS = {"wait_ge", "dma_start", "drain", "sem_inc", "nop", "wait_op"}

    def __init__(self, eng):
        self._e = eng
        self._dirty = []
        self._reads = []

    @staticmethod
    def _span(ap):
        try:
            name = ap.tensor.name
            lo = ap.offset
            dims = list(ap.ap)
            hi = lo + 1
            for step, count in dims[1:]:
                hi += (count - 1) * abs(step)
            return (name, lo, hi)
        except AttributeError:
            return None

    @staticmethod
    def _overlaps(span, lst):
        name, lo, hi = span
        for n, l, h in lst:
            if n == name and lo < h and l < hi:
                return True
        return False

    def __getattr__(self, attr):
        fn = getattr(self._e, attr)
        if attr in AutoDrain._PASS or attr.startswith("_"):
            return fn

        def wrapped(*a, **k):
            outs, ins = [], []
            if "out" in k:
                outs.append(k["out"])
                ins.extend(a)
            elif a:
                outs.append(a[0])
                ins.extend(a[1:])
            if k.get("accum_out") is not None:
                outs.append(k["accum_out"])
            for key, x in k.items():
                if key not in ("out", "accum_out") and x is not None:
                    ins.append(x)
            out_spans = [s for s in (self._span(x) for x in outs) if s]
            in_spans = [s for s in (self._span(x) for x in ins) if s]
            if (any(self._overlaps(s, self._dirty) for s in in_spans + out_spans)
                    or any(self._overlaps(s, self._reads) for s in out_spans)):
                self._e.drain()
                self._dirty = []
                self._reads = []
            inst = fn(*a, **k)
            self._dirty.extend(out_spans)
            self._reads.extend(in_spans)
            return inst

        return wrapped


def build_nc():
    nc = bass.Bass("TRN2", target_bir_lowering=False)

    x_own = nc.dram_tensor("x_own", [HALF, D], F32, kind="ExternalInput").ap()
    x_prev = nc.dram_tensor("x_prev", [HALF, D], F32, kind="ExternalInput").ap()
    wmod_d = nc.dram_tensor("wmod_d", [D, 6 * D], F32, kind="ExternalInput").ap()
    weT_d = nc.dram_tensor("weT_d", [NE, D, D], F32, kind="ExternalInput").ap()
    wgT_d = nc.dram_tensor("wgT_d", [128, 64], F32, kind="ExternalInput").ap()
    be_d = nc.dram_tensor("be_d", [NE, D], F32, kind="ExternalInput").ap()
    cs_d = nc.dram_tensor("cs_d", [128, 8], F32, kind="ExternalInput").ap()
    dab_d = nc.dram_tensor("dab_d", [128, 48], F32, kind="ExternalInput").ap()
    gam_d = nc.dram_tensor("gam_d", [128, 16], F32, kind="ExternalInput").ap()
    omg_d = nc.dram_tensor("omg_d", [128, 8], F32, kind="ExternalInput").ap()
    sgate_d = nc.dram_tensor("sgate_d", [128, 1], F32, kind="ExternalInput").ap()
    out_d = nc.dram_tensor("out_o", [HALF, D], F32, kind="ExternalOutput").ap()

    es = ExitStack()
    sbt = lambda name, shape, dt: es.enter_context(nc.sbuf_tensor(name, shape, dt))
    pst = lambda name, shape, dt: es.enter_context(nc.psum_tensor(name, shape, dt))

    x_sb = sbt("x_sb", [128, NJ, 1024], F32)
    buf_fm = sbt("buf_fm", [128, ND, 2048], BF16)
    web = sbt("web", [128, 2, ND, 1024], BF16)
    wmb = sbt("wmb", [128, 2, ND, 512], BF16)
    t0s = sbt("t0s", [128, 2048], BF16)
    t1s = sbt("t1s", [128, 2048], BF16)
    prevs = sbt("prevs", [128, 2048], BF16)
    h_scr = sbt("h_scr", [128, 2048], BF16)
    res_scr = sbt("res_scr", [128, 2048], BF16)
    xn_scr = sbt("xn_scr", [128, 4, 1024], BF16)
    scr4 = sbt("scr4", [128, 4, 1024], F32)
    stats = sbt("stats", [128, 8, 32], F32)
    gbrow = sbt("gbrow", [1, 1024], F32)
    mstage = gbrow[0:1, 0:512]
    gb_bc = sbt("gb_bc", [128, 1024], BF16)
    rgb_bc = sbt("rgb_bc", [128, 1024], BF16)
    mcols = sbt("mcols", [128, 48], F32)
    tiny = sbt("tiny", [128, 2, 8], F32)
    pq = sbt("pq", [128, 16], F32)
    qq = sbt("qq", [128, 16], F32)
    coef = sbt("coef", [128, 16], F32)
    geo = sbt("geo", [128, 16], F32)
    sxn = sbt("sxn", [128, 16], F32)
    stini = sbt("stini", [128, 16], F32)
    dab = sbt("dab", [128, 48], F32)
    gam = sbt("gam", [128, 16], F32)
    omg = sbt("omg", [128, 8], F32)
    sgate = sbt("sgate", [128, 1], F32)
    cs = sbt("cs", [128, 8], F32)
    csb = sbt("csb", [128, 8], BF16)
    wgT = sbt("wgT", [128, 64], BF16)
    be_bf = sbt("be_bf", [8, 1024], BF16)
    lgs = sbt("lgs", [128, 8], F32)
    m8 = sbt("m8", [128, 8], F32)
    wcol = sbt("wcol", [128, 4], F32)
    cw_all = sbt("cw_all", [128, NJ, 8], F32)
    cwT = sbt("cwT", [8, NJ, 128], BF16)
    ident = sbt("ident", [128, 128], BF16)
    identf = sbt("identf", [128, 128], F32)
    vcst = sbt("vcst", [128, 3], F32)
    ones_r = sbt("ones_r", [1, 128], F32)

    pfn = [pst("pfn0", [128, 1024], F32), pst("pfn1", [128, 1024], F32)]
    ptrs = [pst("ptr", [128, 1024], BF16), pst("ptr2", [128, 1024], BF16),
            pst("ptr3", [128, 1024], BF16)]
    psml = pst("psml", [128, 256], F32)
    pmods = [pfn[0][0:1, 0:512], pfn[1][0:1, 0:512]]
    pcwt = psml[0:8, 128:256]

    sems = {s_: es.enter_context(nc.semaphore(s_)) for s_ in SEM_NAMES}

    seq = []
    for j in range(8):
        seq.append((0, j))
    seq += [("b", j) for j in range(8)]
    for e in range(1, NE):
        seq += [(e, j) for j in range(8)]
    for j in range(8, 16):
        seq.append((7, j))
    seq += [("b", j) for j in range(8, 16)]
    for e in (6, 5, 4, 3, 2, 1, 0):
        seq += [(e, j) for j in range(8, 16)]
    A_LAST = {e: seq.index((e, 7)) for e in range(NE)}
    B_RELOADS = [(e, seq.index((e + 2, 15))) for e in (5, 4, 3, 2, 1, 0)]
    LI = {j: seq.index((7, j)) if j < 8 else seq.index((0, j)) for j in range(NJ)}

    def emit_seq_item(t, em, i):
        it = seq[i]
        g_ = pfn[i % 2]
        if i >= 2:
            em.wait(t, "sv", f"v_grp{i - 2}")
        if it[0] == "b":
            j = it[1]
            em.wait(t, "sc", f"c_cwT{j}")
            if i == 0:
                em.wait(t, "sc", "c_gbbc")
            if i == 1:
                em.wait(t, "sc", "c_rgbbc")
            t.matmul(g_[:, 0:512], cwT[0:8, j, :], be_bf[0:8, 0:512],
                     start=True, stop=True)
            inst = t.matmul(g_[:, 512:1024], cwT[0:8, j, :], be_bf[0:8, 512:1024],
                            start=True, stop=True)
        else:
            e, j = it
            if j == 0:
                em.waitv(t, "swe%d" % e, 16)
            elif j == 8 and e <= 5:
                em.waitv(t, "swe%d" % e, 32)
            for dk in range(ND):
                for hh in range(2):
                    inst = t.matmul(g_[:, 512 * hh:512 * hh + 512],
                                    buf_fm[:, dk, 128 * j:128 * j + 128],
                                    web[:, e % 2, dk, 512 * hh:512 * hh + 512],
                                    start=(dk == 0), stop=(dk == ND - 1))
        em.inc(inst, "st", 1, f"t_grp{i}")

    def stream_gpsimd(g, em):
        for src_ap, dst in [(cs_d, cs), (dab_d, dab), (gam_d, gam), (omg_d, omg),
                            (sgate_d, sgate), (wgT_d, wgT)]:
            inst = g.dma_start(out=dst[:], in_=src_ap)
            em.inc(inst, "sdsm", 16)
        inst = g.dma_start(out=be_bf[:], in_=be_d)
        em.inc(inst, "sdsm", 16, "w_small")
        g.memset(ident[:], 1.0)
        g.memset(ones_r[:], 1.0)
        inst = g.memset(identf[:], 1.0)
        em.inc(inst, "sg", 1, "g_memset")
        em.wait(g, "sg", "g_memset")
        g.affine_select(ident[:], ident[:], [[-1, 128]], ALU.is_equal, 0.0,
                        channel_multiplier=1)
        inst = g.affine_select(identf[:], identf[:], [[-1, 128]], ALU.is_equal, 0.0,
                               channel_multiplier=1)
        em.inc(inst, "sg", 1, "g_ident")
        wm_r = wmod_d.rearrange("(k p) m -> p k m", p=128)
        for cb in range(12):
            if cb >= 2:
                em.wait(g, "st", f"t_gemv{cb - 2}")
            inst = g.dma_start(out=wmb[:, cb % 2], in_=wm_r[:, :, 512 * cb:512 * cb + 512])
            em.inc(inst, ["sdwm", "sdwm1"][cb % 2], 16, f"w_wmod{cb}")
        for e in range(2):
            inst = g.dma_start(out=web[:, e % 2],
                               in_=weT_d[e].rearrange("(k p) o -> p k o", p=128))
            em.inc(inst, "swe%d" % e, 16, f"w_we{e}")
        em.wait(g, "sc", "c_rgbbc")
        for j in range(8):
            em.wait(g, "sc", f"c_xn2_{j}")
            inst = g.tensor_tensor(x_sb[:, j, :], x_sb[:, j, :], rgb_bc[:], ALU.mult)
            em.inc(inst, "sg", 1, f"g_acci{j}")
        for e in range(2, NE):
            em.wait(g, "st", f"t_grp{A_LAST[e - 2]}")
            inst = g.dma_start(out=web[:, e % 2],
                               in_=weT_d[e].rearrange("(k p) o -> p k o", p=128))
            em.inc(inst, "swe%d" % e, 16, f"w_we{e}")
        for j in range(8, NJ):
            em.wait(g, "sc", f"c_xn2_{j}")
            inst = g.tensor_tensor(x_sb[:, j, :], x_sb[:, j, :], rgb_bc[:], ALU.mult)
            em.inc(inst, "sg", 1, f"g_acci{j}")
        for j in range(8):
            em.wait(g, "sv", f"v_grp{LI[j]}")
            if j >= 4:
                em.waitv(g, ["sdo", "sdo1", "sdo2", "sdo3"][j % 4], 16 * (j // 4))
            inst = g.tensor_tensor(scr4[:, j % 4, :], x_sb[:, j, :], gb_bc[:], ALU.mult)
            em.inc(inst, "sg", 1, f"g_fin{j}")
        for e, pred in B_RELOADS:
            em.wait(g, "st", f"t_grp{pred}")
            inst = g.dma_start(out=web[:, e % 2],
                               in_=weT_d[e].rearrange("(k p) o -> p k o", p=128))
            em.inc(inst, "swe%d" % e, 16, f"w_we{e}b")
        for j in range(8, NJ):
            em.wait(g, "sv", f"v_grp{LI[j]}")
            em.waitv(g, ["sdo", "sdo1", "sdo2", "sdo3"][j % 4], 16 * (j // 4))
            inst = g.tensor_tensor(scr4[:, j % 4, :], x_sb[:, j, :], gb_bc[:], ALU.mult)
            em.inc(inst, "sg", 1, f"g_fin{j}")

    def stream_sync(s, em):
        inst = s.dma_start(out=x_sb[:], in_=x_own.rearrange("(c p) d -> p c d", p=128))
        em.inc(inst, "sdx", 16, "x_own")
        for j in range(NJ):
            if j >= 4:
                em.wait(s, "sc", f"c_xnp{j - 4}")
            inst = s.dma_start(out=scr4[:, j % 4, :], in_=x_prev[128 * j:128 * j + 128, :])
            em.inc(inst, ["sdxp", "sdxp1", "sdxp2", "sdxp3"][j % 4], 16, f"xp{j}")
        for j in range(NJ):
            em.wait(s, "sg", f"g_fin{j}")
            inst = s.dma_start(out=out_d[128 * j:128 * j + 128, :], in_=scr4[:, j % 4, :])
            em.inc(inst, ["sdo", "sdo1", "sdo2", "sdo3"][j % 4], 16, f"o{j}")
        for r in ("sdo", "sdo1", "sdo2", "sdo3"):
            em.waitv(s, r, 16 * 4)

    def stream_scalar(c, em):
        em.wait(c, "sdsm", "w_small")
        inst = c.activation(out=csb[:], in_=cs[:], func=AF.Silu)
        em.inc(inst, "sc", 1, "c_silu_cs")
        c.activation(out=pq[:], in_=dab[:, 0:16], func=AF.Sigmoid)
        inst = c.activation(out=geo[:], in_=dab[:, 16:32], func=AF.Sigmoid)
        em.inc(inst, "sc", 1, "c_sigs")
        em.wait(c, "sv", "v_qq")
        inst = c.activation(out=geo[:], in_=qq[:], func=AF.Ln)
        em.inc(inst, "sc", 1, "c_lnq")
        em.wait(c, "sv", "v_coef")
        inst = c.activation(out=pq[:], in_=geo[:], func=AF.Exp, scale=vcst[:, 1:2])
        em.inc(inst, "sc", 1, "c_q2048")

        for j in range(NJ):
            em.wait(c, "sv", f"v_pstat{j}")
            c.activation(out=stats[:, 7, j:j + 1], in_=stats[:, 0, j:j + 1],
                         func=AF.Square, scale=RD)
            c.mul(out=stats[:, 7, j:j + 1], in_=stats[:, 7, j:j + 1], mul=-1.0)
            c.activation(out=stats[:, 3, j:j + 1], in_=stats[:, 1, j:j + 1],
                         func=AF.Identity, scale=RD)
            c.add(out=stats[:, 3, j:j + 1], in_=stats[:, 3, j:j + 1], add=stats[:, 7, j:j + 1])
            c.activation(out=stats[:, 4, j:j + 1], in_=stats[:, 3, j:j + 1],
                         func=AF.Ln, bias=vcst[:, 0:1])
            c.activation(out=stats[:, 5, j:j + 1], in_=stats[:, 4, j:j + 1],
                         func=AF.Exp, scale=vcst[:, 2:3])
            c.activation(out=stats[:, 6, j:j + 1], in_=stats[:, 0, j:j + 1],
                         func=AF.Identity, scale=stats[:, 5, j:j + 1])
            c.mul(out=stats[:, 6, j:j + 1], in_=stats[:, 6, j:j + 1], mul=-RD)
            inst = c.activation(out=xn_scr[:, j % 4, :], in_=scr4[:, j % 4, :], func=AF.Identity,
                                scale=stats[:, 5, j:j + 1], bias=stats[:, 6, j:j + 1])
            em.inc(inst, "sc", 1, f"c_xnp{j}")
            em.wait(c, "st", f"t_trp{j}")
            inst = c.copy(out=buf_fm[:, :, 128 * j:128 * j + 128],
                          in_=ptrs[j % 3].ap().rearrange("p (k t) -> p k t", k=ND))
            em.inc(inst, "sc", 1, f"c_evp{j}")
            if 2 <= j < 14:
                cb = j - 2
                em.wait(c, "st", f"t_gemv{cb}")
                if cb >= 2:
                    em.wait(c, "st", f"t_k1_{cb - 2}")
                inst = c.copy(out=gbrow[0:1, 512 * (cb % 2):512 * (cb % 2) + 512],
                              in_=pmods[cb % 2])
                em.inc(inst, "sc", 1, f"c_gemv{cb}")
        em.wait(c, "st", "t_k1_9")
        inst = c.copy(out=mcols[:, 0:40], in_=psml[:, 0:40])
        em.inc(inst, "sc", 1, "c_mcols")

        em.wait(c, "sv", "v_stato")
        c.activation(out=stats[:, 4, 16:32], in_=stats[:, 3, 16:32],
                     func=AF.Ln, bias=vcst[:, 0:1])
        c.activation(out=stats[:, 5, 16:32], in_=stats[:, 4, 16:32],
                     func=AF.Exp, scale=vcst[:, 2:3])
        for j in range(NJ):
            c.activation(out=stats[:, 6, 16 + j:17 + j], in_=stats[:, 2, 16 + j:17 + j],
                         func=AF.Identity, scale=stats[:, 5, 16 + j:17 + j])
            c.mul(out=stats[:, 6, 16 + j:17 + j], in_=stats[:, 6, 16 + j:17 + j], mul=-1.0)
        for j in range(NJ):
            if j >= 4:
                em.wait(c, "st", f"t_tro{j - 4}")
            inst = c.activation(out=xn_scr[:, j % 4, :], in_=x_sb[:, j, :], func=AF.Identity,
                                scale=stats[:, 5, 16 + j:17 + j], bias=stats[:, 6, 16 + j:17 + j])
            em.inc(inst, "sc", 1, f"c_xno{j}")
            if j >= 3:
                em.wait(c, "st", f"t_tro{j - 3}")
                if j == 3:
                    em.wait(c, "sv", "v_sxn_all")
                inst = c.copy(out=buf_fm[:, :, 128 * (j - 3):128 * (j - 3) + 128],
                              in_=ptrs[(j - 3) % 3].ap().rearrange("p (k t) -> p k t", k=ND))
                em.inc(inst, "sc", 1, f"c_evo{j - 3}")
        for jj in (NJ - 3, NJ - 2, NJ - 1):
            em.wait(c, "st", f"t_tro{jj}")
            inst = c.copy(out=buf_fm[:, :, 128 * jj:128 * jj + 128],
                          in_=ptrs[jj % 3].ap().rearrange("p (k t) -> p k t", k=ND))
            em.inc(inst, "sc", 1, f"c_evo{jj}")
        em.wait(c, "st", "t_gbbc")
        inst = c.copy(out=gb_bc[:], in_=pfn[0][:])
        em.inc(inst, "sc", 1, "c_gbbc")
        em.wait(c, "st", "t_rgbbc")
        inst = c.copy(out=rgb_bc[:], in_=pfn[1][:])
        em.inc(inst, "sc", 1, "c_rgbbc")

        for dk in range(ND):
            em.wait(c, "sv", f"v_pre{dk}")
            if dk >= 1:
                em.wait(c, "sv", f"v_dfm{dk - 1}")
            inst = c.activation(out=res_scr[:], in_=prevs[:], func=AF.Silu)
            em.inc(inst, "sc", 1, f"c_sil{dk}")

        for j in range(NJ):
            em.wait(c, "sv", f"v_x2_{j}")
            inst = c.activation(out=scr4[:, j % 2, :], in_=x_sb[:, j, :], func=AF.Square,
                                accum_out=stats[:, 1, 16 + j:17 + j])
            if j == NJ - 1:
                em.inc(inst, "sc", 1, "c_sq2")
        em.wait(c, "sv", "v_stat2")
        c.activation(out=stats[:, 4, 16:32], in_=stats[:, 3, 16:32],
                     func=AF.Ln, bias=vcst[:, 0:1])
        c.activation(out=stats[:, 5, 16:32], in_=stats[:, 4, 16:32],
                     func=AF.Exp, scale=vcst[:, 2:3])
        for j in range(NJ):
            c.activation(out=stats[:, 6, 16 + j:17 + j], in_=stats[:, 2, 16 + j:17 + j],
                         func=AF.Identity, scale=stats[:, 5, 16 + j:17 + j])
            c.mul(out=stats[:, 6, 16 + j:17 + j], in_=stats[:, 6, 16 + j:17 + j], mul=-1.0)

        def _h2_chunk(jj):
            em.wait(c, "st", f"t_trx{jj}")
            for dk in range(ND):
                inst = c.activation(out=buf_fm[:, dk, 128 * jj:128 * jj + 128],
                                    in_=ptrs[jj % 3][:, 128 * dk:128 * dk + 128], func=AF.Identity,
                                    scale=mcols[:, 32 + dk:33 + dk],
                                    bias=mcols[:, 24 + dk:25 + dk])
            em.inc(inst, "sc", 1, f"c_evh{jj}")
            if jj >= 2:
                em.wait(c, "st", f"t_cwT{jj - 2}")
                inst = c.copy(out=cwT[0:8, jj - 2, :], in_=pcwt)
                em.inc(inst, "sc", 1, f"c_cwT{jj - 2}")
            em.wait(c, "st", f"t_lg{jj}")
            if jj >= 1:
                em.wait(c, "sv", f"v_cw{jj - 1}")
            inst = c.copy(out=lgs[:], in_=psml[:, 48:56])
            em.inc(inst, "sc", 1, f"c_lg{jj}")
            em.wait(c, "sv", f"v_d21_{jj}")
            inst = c.activation(out=wcol[:, 1:2], in_=wcol[:, 0:1], func=AF.Exp)
            em.inc(inst, "sc", 1, f"c_exp{jj}")

        for j in range(NJ):
            if j >= 4:
                em.wait(c, "st", f"t_trx{j - 4}")
            inst = c.activation(out=xn_scr[:, j % 4, :], in_=x_sb[:, j, :], func=AF.Identity,
                                scale=stats[:, 5, 16 + j:17 + j], bias=stats[:, 6, 16 + j:17 + j])
            em.inc(inst, "sc", 1, f"c_xn2_{j}")
            if j >= 1:
                _h2_chunk(j - 1)
        _h2_chunk(NJ - 1)
        for jj in (NJ - 2, NJ - 1):
            em.wait(c, "st", f"t_cwT{jj}")
            inst = c.copy(out=cwT[0:8, jj, :], in_=pcwt)
            em.inc(inst, "sc", 1, f"c_cwT{jj}")

    def stream_tensor(t, em):
        em.wait(t, "sg", "g_ident")
        em.wait(t, "sc", "c_silu_cs")
        for j in range(NJ):
            em.wait(t, "sc", f"c_xnp{j}")
            if j >= 3:
                em.wait(t, "sc", f"c_evp{j - 3}")
            for dk in range(ND):
                inst = t.transpose(ptrs[j % 3][:, 128 * dk:128 * dk + 128],
                                   xn_scr[:, j % 4, 128 * dk:128 * dk + 128], ident[:])
            em.inc(inst, "st", 1, f"t_trp{j}")
            if 3 <= j <= 12:
                pb = j - 3
                base = 512 * (pb % 2)
                for i in range(4):
                    m = 4 * pb + i
                    inst = t.matmul(psml[:, m:m + 1],
                                    gbrow[0:1, base + 128 * i:base + 128 * i + 128],
                                    ones_r[0:1, 0:1], start=True, stop=True)
                em.inc(inst, "st", 1, f"t_k1_{pb}")
            if j < 12:
                cb = j
                em.waitv(t, ["sdwm", "sdwm1"][cb % 2], 16 * (cb // 2 + 1))
                if cb >= 2:
                    em.wait(t, "sc", f"c_gemv{cb - 2}")
                for dk in range(ND):
                    inst = t.matmul(pmods[cb % 2], csb[:, dk:dk + 1], wmb[:, cb % 2, dk, :],
                                    start=(dk == 0), stop=(dk == ND - 1))
                em.inc(inst, "st", 1, f"t_gemv{cb}")
        em.wait(t, "sv", "v_gb1")
        t.matmul(pfn[0][:, 0:512], ones_r[0:1, :], gbrow[0:1, 0:512], start=True, stop=True)
        inst = t.matmul(pfn[0][:, 512:1024], ones_r[0:1, :], gbrow[0:1, 512:1024],
                        start=True, stop=True)
        em.inc(inst, "st", 1, "t_gbbc")
        em.wait(t, "sv", "v_rgb2")
        t.matmul(pfn[1][:, 0:512], ones_r[0:1, :], gbrow[0:1, 0:512], start=True, stop=True)
        inst = t.matmul(pfn[1][:, 512:1024], ones_r[0:1, :], gbrow[0:1, 512:1024],
                        start=True, stop=True)
        em.inc(inst, "st", 1, "t_rgbbc")
        for j in range(NJ):
            em.wait(t, "sc", f"c_xno{j}")
            em.wait(t, "sc", f"c_evp{13 + j}" if j < 3 else f"c_evo{j - 3}")
            for dk in range(ND):
                inst = t.transpose(ptrs[j % 3][:, 128 * dk:128 * dk + 128],
                                   xn_scr[:, j % 4, 128 * dk:128 * dk + 128], ident[:])
            em.inc(inst, "st", 1, f"t_tro{j}")
        for j in range(NJ):
            if j == 0:
                em.wait(t, "sv", "v_dfm7")
            elif j >= 3:
                em.wait(t, "sv", f"v_x2_{j - 3}")
            for dk in range(ND):
                inst = t.transpose(ptrs[j % 3][:, 128 * dk:128 * dk + 128],
                                   buf_fm[:, dk, 128 * j:128 * j + 128], ident[:])
            em.inc(inst, "st", 1, f"t_trd{j}")

        def _logits_chunk(jj):
            em.wait(t, "sc", f"c_evh{jj}")
            em.wait(t, "sc", "c_mcols" if jj == 0 else f"c_lg{jj - 1}")
            if jj >= 2:
                em.wait(t, "sc", f"c_cwT{jj - 2}")
            for dk in range(ND):
                inst = t.matmul(psml[:, 48:56], buf_fm[:, dk, 128 * jj:128 * jj + 128],
                                wgT[:, 8 * dk:8 * dk + 8],
                                start=(dk == 0), stop=(dk == ND - 1))
            em.inc(inst, "st", 1, f"t_lg{jj}")

        def _cwt_chunk(jj):
            em.wait(t, "sv", f"v_cw{jj}")
            if jj >= 1:
                em.wait(t, "sc", f"c_cwT{jj - 1}")
            inst = t.transpose(pcwt, cw_all[:, jj, :], identf[:])
            em.inc(inst, "st", 1, f"t_cwT{jj}")

        sq_i = 0
        for j in range(NJ):
            em.wait(t, "sc", f"c_xn2_{j}")
            if j < 3:
                em.wait(t, "sv", f"v_x2_{[15, 13, 14][j]}")
            if j >= 3:
                em.wait(t, "sc", f"c_evh{j - 3}")
            for dk in range(ND):
                inst = t.transpose(ptrs[j % 3][:, 128 * dk:128 * dk + 128],
                                   xn_scr[:, j % 4, 128 * dk:128 * dk + 128], ident[:])
            em.inc(inst, "st", 1, f"t_trx{j}")
            if j >= 1:
                _logits_chunk(j - 1)
            if j >= 2:
                _cwt_chunk(j - 2)
            emitted = 0
            while sq_i < len(seq) and seq[sq_i][1] <= j - 2 and emitted < 8:
                emit_seq_item(t, em, sq_i)
                sq_i += 1
                emitted += 1
        _logits_chunk(NJ - 1)
        _cwt_chunk(NJ - 2)
        _cwt_chunk(NJ - 1)
        while sq_i < len(seq):
            emit_seq_item(t, em, sq_i)
            sq_i += 1

    def stream_vector(v, em):
        v.memset(vcst[:, 0:1], EPS)
        v.memset(vcst[:, 1:2], 2048.0)
        v.memset(vcst[:, 2:3], -0.5)
        em.wait(v, "sc", "c_sigs")
        v.tensor_tensor(stini[:], pq[:], geo[:], ALU.mult)
        inst = v.tensor_scalar(qq[:], stini[:], -1.0, 1.0, ALU.mult, ALU.add)
        em.inc(inst, "sv", 1, "v_qq")
        v.tensor_tensor(sxn[:], pq[:], dab[:, 32:48], ALU.mult)
        v.tensor_tensor(coef[:], sxn[:], gam[:], ALU.mult)
        inst = v.tensor_scalar(coef[:], coef[:], SCALE, None, ALU.mult)
        em.inc(inst, "sv", 1, "v_coef")
        em.wait(v, "sc", "c_q2048")
        v.tensor_scalar(pq[:], pq[:], -1.0, 1.0, ALU.mult, ALU.add)
        v.tensor_scalar(stini[:], qq[:], -1.0, 1.0, ALU.mult, ALU.add)
        v.reciprocal(stini[:], stini[:])
        inst = v.tensor_tensor(geo[:], pq[:], stini[:], ALU.mult)
        em.inc(inst, "sv", 1, "v_geo")

        dumps = [prevs[:, 0:1024], prevs[:, 1024:2048], t0s[:, 0:1024], t0s[:, 1024:2048],
                 t1s[:, 0:1024], t1s[:, 1024:2048],
                 h_scr[:, 0:1024], h_scr[:, 1024:2048], res_scr[:, 0:1024], res_scr[:, 1024:2048]]
        for j in range(NJ):
            em.waitv(v, ["sdxp", "sdxp1", "sdxp2", "sdxp3"][j % 4], 16 * (j // 4 + 1))
            v.scalar_tensor_tensor(dumps[j % 10], scr4[:, j % 4, :], 1.0,
                                   scr4[:, j % 4, :], ALU.mult, ALU.mult,
                                   accum_out=stats[:, 1, j:j + 1])
            inst = v.reduce_sum(out=stats[:, 0, j:j + 1], in_=scr4[:, j % 4, :],
                                axis=mybir.AxisListType.X)
            em.inc(inst, "sv", 1, f"v_pstat{j}")

        em.wait(v, "sdx", "x_own")
        for j in range(NJ):
            v.reduce_sum(out=stats[:, 0, 16 + j:17 + j], in_=x_sb[:, j, :],
                         axis=mybir.AxisListType.X)
            v.scalar_tensor_tensor(dumps[j % 10], x_sb[:, j, :], 1.0, x_sb[:, j, :],
                                   ALU.mult, ALU.mult, accum_out=stats[:, 1, 16 + j:17 + j])
        v.tensor_scalar(stats[:, 2, 16:32], stats[:, 0, 16:32], RD, None, ALU.mult)
        v.tensor_scalar(stats[:, 4, 16:32], stats[:, 1, 16:32], RD, None, ALU.mult)
        v.tensor_tensor(stats[:, 7, 16:32], stats[:, 2, 16:32], stats[:, 2, 16:32], ALU.mult)
        inst = v.tensor_tensor(stats[:, 3, 16:32], stats[:, 4, 16:32], stats[:, 7, 16:32],
                               ALU.subtract)
        em.inc(inst, "sv", 1, "v_stato")

        em.wait(v, "sc", "c_mcols")
        v.tensor_scalar(mcols[:, 8:16], mcols[:, 8:16], 1.0, None, ALU.add)
        v.tensor_scalar(mcols[:, 32:40], mcols[:, 32:40], 1.0, None, ALU.add)
        v.tensor_scalar(tiny[:, 0, :], mcols[:, 16:24], 0.0, None, ALU.is_ge)
        v.tensor_scalar(tiny[:, 0, :], tiny[:, 0, :], 2.0, -1.0, ALU.mult, ALU.add)
        v.tensor_scalar(tiny[:, 1, :], mcols[:, 16:24], -1.0, None, ALU.mult)
        v.tensor_tensor(tiny[:, 1, :], tiny[:, 1, :], mcols[:, 16:24], ALU.max)
        v.tensor_scalar(tiny[:, 1, :], tiny[:, 1, :], 0.25, None, ALU.max)
        inst = v.tensor_tensor(mcols[:, 16:24], tiny[:, 0, :], tiny[:, 1, :], ALU.mult)
        em.inc(inst, "sv", 1, "v_ca1")
        em.wait(v, "sc", "c_gemv11")
        v.tensor_scalar(rgb_bc[0:1, :], gbrow[0:1, :], 0.0, None, ALU.is_ge)
        v.tensor_scalar(rgb_bc[0:1, :], rgb_bc[0:1, :], 2.0, -1.0, ALU.mult, ALU.add)
        v.tensor_scalar(gb_bc[0:1, :], gbrow[0:1, :], -1.0, None, ALU.mult)
        v.tensor_tensor(gb_bc[0:1, :], gb_bc[0:1, :], gbrow[0:1, :], ALU.max)
        v.tensor_scalar(gb_bc[0:1, :], gb_bc[0:1, :], 0.25, None, ALU.max)
        inst = v.tensor_tensor(gbrow[0:1, :], rgb_bc[0:1, :], gb_bc[0:1, :], ALU.mult)
        em.inc(inst, "sv", 1, "v_gb1")

        em.wait(v, "st", "t_gbbc")
        inst = v.reciprocal(gbrow[0:1, :], gbrow[0:1, :])
        em.inc(inst, "sv", 1, "v_rgb2")

        em.wait(v, "sc", f"c_evp{NJ - 1}")
        for dk in range(ND):
            for n in range(2):
                v.tensor_tensor_scan(out=prevs[:], data0=qq[:, 2 * dk + n:2 * dk + n + 1].to_broadcast([128, 2048]),
                                     data1=buf_fm[:, dk, :], initial=0.0,
                                     op0=ALU.mult, op1=ALU.add)
                inst = v.tensor_copy(sxn[:, 2 * dk + n:2 * dk + n + 1], prevs[:, 2047:2048])
        em.inc(inst, "sv", 1, "v_vsc")

        for dk in range(ND):
            sl = slice(2 * dk, 2 * dk + 2)
            v.tensor_scalar(stini[:, sl], sxn[:, sl], mcols[:, 8 + dk:9 + dk], None, ALU.mult)
            v.scalar_tensor_tensor(stini[:, sl], geo[:, sl], mcols[:, dk:dk + 1],
                                   stini[:, sl], ALU.mult, ALU.add)
            inst = v.tensor_scalar(stini[:, sl], stini[:, sl], sgate[:, 0:1], None, ALU.mult)
        em.inc(inst, "sv", 1, "v_sxn_all")
        em.wait(v, "sc", f"c_evo{NJ - 1}")
        for dk in range(ND):
            v.tensor_scalar(h_scr[:], buf_fm[:, dk, :], mcols[:, 8 + dk:9 + dk],
                            mcols[:, dk:dk + 1], ALU.mult, ALU.add)
            v.tensor_tensor_scan(out=t0s[:], data0=qq[:, 2 * dk:2 * dk + 1].to_broadcast([128, 2048]),
                                 data1=h_scr[:], initial=stini[:, 2 * dk:2 * dk + 1],
                                 op0=ALU.mult, op1=ALU.add)
            v.tensor_tensor_scan(out=t1s[:], data0=qq[:, 2 * dk + 1:2 * dk + 2].to_broadcast([128, 2048]),
                                 data1=h_scr[:], initial=stini[:, 2 * dk + 1:2 * dk + 2],
                                 op0=ALU.mult, op1=ALU.add)
            if dk >= 1:
                em.wait(v, "sc", f"c_sil{dk - 1}")
                inst = v.tensor_scalar(buf_fm[:, dk - 1, :], res_scr[:],
                                       mcols[:, 16 + dk - 1:17 + dk - 1], None, ALU.mult)
                em.inc(inst, "sv", 1, f"v_dfm{dk - 1}")
            v.tensor_scalar(res_scr[:], h_scr[:], omg[:, dk:dk + 1], None, ALU.mult)
            v.scalar_tensor_tensor(t0s[:], t0s[:], coef[:, 2 * dk:2 * dk + 1], res_scr[:],
                                   ALU.mult, ALU.add)
            inst = v.scalar_tensor_tensor(prevs[:], t1s[:], coef[:, 2 * dk + 1:2 * dk + 2],
                                          t0s[:], ALU.mult, ALU.add)
            em.inc(inst, "sv", 1, f"v_pre{dk}")
        em.wait(v, "sc", f"c_sil{ND - 1}")
        inst = v.tensor_scalar(buf_fm[:, ND - 1, :], res_scr[:],
                               mcols[:, 16 + ND - 1:17 + ND - 1], None, ALU.mult)
        em.inc(inst, "sv", 1, f"v_dfm{ND - 1}")

        for j in range(NJ):
            em.wait(v, "st", f"t_trd{j}")
            inst = v.tensor_tensor(x_sb[:, j, :], x_sb[:, j, :], ptrs[j % 3][:], ALU.add)
            em.inc(inst, "sv", 1, f"v_x2_{j}")
            v.reduce_sum(out=stats[:, 0, 16 + j:17 + j], in_=x_sb[:, j, :],
                         axis=mybir.AxisListType.X)
        em.wait(v, "sc", "c_sq2")
        v.tensor_scalar(stats[:, 2, 16:32], stats[:, 0, 16:32], RD, None, ALU.mult)
        v.tensor_scalar(stats[:, 4, 16:32], stats[:, 1, 16:32], RD, None, ALU.mult)
        v.tensor_tensor(stats[:, 7, 16:32], stats[:, 2, 16:32], stats[:, 2, 16:32], ALU.mult)
        inst = v.tensor_tensor(stats[:, 3, 16:32], stats[:, 4, 16:32], stats[:, 7, 16:32],
                               ALU.subtract)
        em.inc(inst, "sv", 1, "v_stat2")

        def _comb_item(i):
            it = seq[i]
            g_ = pfn[i % 2]
            em.wait(v, "st", f"t_grp{i}")
            if it[0] == "b":
                jj = it[1]
                em.wait(v, "sg", f"g_acci{jj}")
                inst = v.scalar_tensor_tensor(x_sb[:, jj, :], g_[:], 1.0, x_sb[:, jj, :],
                                              ALU.mult, ALU.add)
            else:
                e, jj = it
                em.wait(v, "sg", f"g_acci{jj}")
                inst = v.scalar_tensor_tensor(x_sb[:, jj, :], g_[:],
                                              cw_all[:, jj, e:e + 1], x_sb[:, jj, :],
                                              ALU.mult, ALU.add)
            em.inc(inst, "sv", 1, f"v_grp{i}")

        cb_i = 0
        for j in range(NJ):
            emitted = 0
            while cb_i < len(seq) and seq[cb_i][1] <= j - 2 and emitted < 8:
                _comb_item(cb_i)
                cb_i += 1
                emitted += 1
            em.wait(v, "sc", f"c_lg{j}")
            v.max(out=m8[:], in_=lgs[:])
            inst = v.tensor_tensor(wcol[:, 0:1], m8[:, 1:2], m8[:, 0:1], ALU.subtract)
            em.inc(inst, "sv", 1, f"v_d21_{j}")
            em.wait(v, "sc", f"c_exp{j}")
            v.tensor_scalar(wcol[:, 2:3], wcol[:, 1:2], 1.0, None, ALU.add)
            v.reciprocal(wcol[:, 2:3], wcol[:, 2:3])
            v.tensor_tensor(wcol[:, 3:4], wcol[:, 1:2], wcol[:, 2:3], ALU.mult)
            v.tensor_scalar(cw_all[:, j, :], lgs[:], m8[:, 0:1], None, ALU.is_equal)
            v.tensor_scalar(cw_all[:, j, :], cw_all[:, j, :], wcol[:, 2:3], None, ALU.mult)
            v.tensor_scalar(tiny[:, 0, :], lgs[:], m8[:, 1:2], None, ALU.is_equal)
            inst = v.scalar_tensor_tensor(cw_all[:, j, :], tiny[:, 0, :], wcol[:, 3:4],
                                          cw_all[:, j, :], ALU.mult, ALU.add)
            em.inc(inst, "sv", 1, f"v_cw{j}")

        while cb_i < len(seq):
            _comb_item(cb_i)
            cb_i += 1

    streams = [stream_gpsimd, stream_sync, stream_scalar, stream_tensor, stream_vector]
    plan = Plan()
    for fn in streams:
        fn(DummyEng(), plan)
    em = Emit(plan, sems)

    with nc.Block() as block:
        @block.gpsimd
        def _(g):
            stream_gpsimd(AutoDrain(g), em)

        @block.sync
        def _(s):
            stream_sync(s, em)

        @block.scalar
        def _(c):
            stream_scalar(AutoDrain(c), em)

        @block.tensor
        def _(t):
            stream_tensor(t, em)

        @block.vector
        def _(v):
            stream_vector(AutoDrain(v), em)

    es.close()
    return nc


_NC = None


def _get_nc():
    global _NC
    if _NC is None:
        _NC = build_nc()
    return _NC


def _pack_cols(a):
    k = a.shape[1]
    return np.ascontiguousarray(a.reshape(8, 128, k).transpose(1, 0, 2).reshape(128, 8 * k))


def kernel(L0_attn_x, L0_attn_delta, L0_attn_alpha, L0_attn_beta, L0_attn_gamma,
           L0_attn_omega, L0_ffn_Wg, L0_ffn_We, L0_ffn_be, L0_c, L0_wmod):
    x = np.ascontiguousarray(np.asarray(L0_attn_x, dtype=np.float32))
    delta = np.asarray(L0_attn_delta, dtype=np.float32).reshape(D, 2)
    alpha = np.asarray(L0_attn_alpha, dtype=np.float32).reshape(D, 2)
    beta = np.asarray(L0_attn_beta, dtype=np.float32).reshape(D, 2)
    gamma = np.asarray(L0_attn_gamma, dtype=np.float32).reshape(D, 2)
    omega = np.asarray(L0_attn_omega, dtype=np.float32).reshape(D, 1)
    Wg = np.asarray(L0_ffn_Wg, dtype=np.float32)
    We = np.asarray(L0_ffn_We, dtype=np.float32)
    be = np.ascontiguousarray(np.asarray(L0_ffn_be, dtype=np.float32))
    cc = np.asarray(L0_c, dtype=np.float32)
    wmod = np.ascontiguousarray(np.asarray(L0_wmod, dtype=np.float32))

    weT = np.ascontiguousarray(We.transpose(0, 2, 1))
    wgT = np.ascontiguousarray(Wg.T.reshape(8, 128, 8).transpose(1, 0, 2).reshape(128, 64))
    dab = np.ascontiguousarray(np.concatenate(
        [_pack_cols(delta), _pack_cols(alpha), _pack_cols(beta)], axis=1))
    gam = _pack_cols(gamma)
    omg = _pack_cols(omega)

    in_maps = []
    for c_id in range(8):
        b, h = c_id // 2, c_id % 2
        in_maps.append({
            "x_own": np.ascontiguousarray(x[b, HALF * h:HALF * (h + 1), :]),
            "x_prev": np.ascontiguousarray(x[b, 0:HALF, :]),
            "wmod_d": wmod,
            "weT_d": weT,
            "wgT_d": wgT,
            "be_d": be,
            "cs_d": np.ascontiguousarray(cc[b].reshape(8, 128).T),
            "dab_d": dab,
            "gam_d": gam,
            "omg_d": omg,
            "sgate_d": np.full((128, 1), float(h), dtype=np.float32),
        })

    nc = _get_nc()
    res = run_bass_kernel_spmd(nc, in_maps, core_ids=list(range(8)))
    out = np.empty((B, S, D), dtype=np.float32)
    for c_id in range(8):
        b, h = c_id // 2, c_id % 2
        out[b, HALF * h:HALF * (h + 1), :] = res.results[c_id]["out_o"]
    return out
